# version 01
# baseline (speedup 1.0000x reference)
import contextlib
import numpy as np
import ml_dtypes
import concourse.bass as bass
import concourse.mybir as mybir
from concourse.alu_op_type import AluOpType as OP
from concourse.bass_utils import run_bass_kernel_spmd

F32 = mybir.dt.float32
BF16 = mybir.dt.bfloat16
I32 = mybir.dt.int32
AF = mybir.ActivationFunctionType
AX = mybir.AxisListType.X
BF = ml_dtypes.bfloat16

NCORE = 8
P = 128
S = 8192
D = 4096
BS = 512
NB = S // BS
NKV = 8
NQH = 32
HD = 128
SCALE = 1.0 / np.sqrt(128.0)
EPS = 1e-5


def _own_blocks(i):
    return (i, NB - 1 - i)


def _own_rows(i):
    a, b = _own_blocks(i)
    return np.r_[a * BS:(a + 1) * BS, b * BS:(b + 1) * BS]


def _rope_tables():
    inv_freq = 1.0 / (10000.0 ** (np.arange(0, HD, 2, dtype=np.float32) / HD))
    pos = np.arange(S, dtype=np.float32)
    fr = np.outer(pos, inv_freq.astype(np.float64)).astype(np.float64)
    cosf = np.cos(fr).astype(np.float32)
    sinf = np.sin(fr).astype(np.float32)
    idxK = np.r_[0:32, 0:32]
    cosK64 = cosf[:, idxK].T
    sinK64 = sinf[:, idxK].T
    sgn = np.where(np.arange(64) % 64 < 32, -1.0, 1.0)[:, None].astype(np.float32)
    cosK = np.concatenate([cosK64, cosK64], 0)
    sinKS = np.concatenate([sinK64 * sgn, sinK64 * sgn], 0)
    fq = np.r_[np.arange(64) % 32, 32 + np.arange(64) % 32]
    cosQ = cosf[:, fq].T
    sgq = np.where(np.arange(128) % 64 < 32, -1.0, 1.0)[:, None].astype(np.float32)
    sinQS = sinf[:, fq].T * sgq
    return (np.ascontiguousarray(cosK), np.ascontiguousarray(sinKS),
            np.ascontiguousarray(cosQ), np.ascontiguousarray(sinQS))


_PK = np.r_[0:32, 64:96, 32:64, 96:128]


def _wq_perm_T(Wq):
    perm = (np.arange(NQH)[:, None] * HD + _PK[None, :]).ravel()
    return np.ascontiguousarray(Wq[perm].T)


def _stream_oc(W, oc_size=128):
    O, K = W.shape
    W4 = W.reshape(O // oc_size, oc_size, K // P, P)
    return np.ascontiguousarray(W4.transpose(0, 2, 3, 1))


def _tileT(M):
    R, C = M.shape
    return np.ascontiguousarray(M.reshape(R // P, P, C // P, P).transpose(2, 0, 3, 1))


class Sem:
    def __init__(self, stk, nc, name):
        self.h = stk.enter_context(nc.semaphore(name))
        self.n = 0

    def inc(self, inst, k=1):
        inst.then_inc(self.h, k)
        self.n += k
        return self.n

    def wait(self, eng, n=None):
        eng.wait_ge(self.h, self.n if n is None else n)


class Prog:

    def __init__(self, name):
        self.nc = bass.Bass("TRN2", target_bir_lowering=False)
        self.stk = contextlib.ExitStack()
        self.name = name

    def __enter__(self):
        self.stk.__enter__()
        nc, stk = self.nc, self.stk
        self.ld = Sem(stk, nc, "s_ld")
        self.lw = Sem(stk, nc, "s_lw")
        self.mm = Sem(stk, nc, "s_mm")
        self.ac = Sem(stk, nc, "s_ac")
        self.vv = Sem(stk, nc, "s_vv")
        self.st = Sem(stk, nc, "s_st")
        return self

    def __exit__(self, *a):
        return self.stk.__exit__(*a)

    def sb(self, name, shape, dtype):
        return self.stk.enter_context(self.nc.sbuf_tensor(name, list(shape), dtype))

    def ps(self, name, shape, dtype=F32):
        return self.stk.enter_context(self.nc.psum_tensor(name, list(shape), dtype))

    def inp(self, name, shape, dtype):
        return self.nc.dram_tensor(name, list(shape), dtype, kind="ExternalInput").ap()

    def out(self, name, shape, dtype):
        return self.nc.dram_tensor(name, list(shape), dtype, kind="ExternalOutput").ap()

    @property
    def sy(self):
        return self.nc.sync

    @property
    def g(self):
        return self.nc.gpsimd

    @property
    def v(self):
        return self.nc.vector

    @property
    def a(self):
        return self.nc.scalar

    @property
    def t(self):
        return self.nc.tensor

    def vc(self, inst):
        self.vv.inc(inst)
        self.v.wait_ge(self.vv.h, self.vv.n)
        return inst

    def ach(self, inst):
        self.ac.inc(inst)
        self.a.wait_ge(self.ac.h, self.ac.n)
        return inst

    def finish(self):
        self.sy.wait_ge(self.st.h, self.st.n)


def _consts(pr):
    c = {}
    c["I_bf"] = pr.sb("cI_bf", (P, P), BF16)
    c["I_f"] = pr.sb("cI_f", (P, P), F32)
    c["ones_f"] = pr.sb("cones_f", (P, 1), F32)
    c["ones_bf"] = pr.sb("cones_bf", (P, 1), BF16)
    c["one1_f"] = pr.sb("cone1_f", (1, P), F32)
    c["one1_bf"] = pr.sb("cone1_bf", (1, P), BF16)
    c["E0_bf"] = pr.sb("cE0_bf", (P, P), BF16)
    c["onesm_bf"] = pr.sb("conesm_bf", (P, P), BF16)
    c["eps"] = pr.sb("ceps", (P, 1), F32)
    return c


def _load_consts(pr, c, CON):
    pass


def _const_inputs(pr):
    c = _consts(pr)
    iI = pr.inp("C_I", (P, P), F32)
    pr.sy.dma_start(out=c["I_f"][:], in_=iI).then_inc(pr.ld.h, 16)
    pr.ld.n += 16
    pr.g.dma_start(out=c["I_bf"][:], in_=iI).then_inc(pr.lw.h, 16)
    pr.lw.n += 16
    iO = pr.inp("C_ones", (P, 1), F32)
    pr.sy.dma_start(out=c["ones_f"][:], in_=iO).then_inc(pr.ld.h, 16)
    pr.ld.n += 16
    pr.g.dma_start(out=c["ones_bf"][:], in_=iO).then_inc(pr.lw.h, 16)
    pr.lw.n += 16
    i1 = pr.inp("C_one1", (1, P), F32)
    pr.sy.dma_start(out=c["one1_f"][:], in_=i1).then_inc(pr.ld.h, 16)
    pr.ld.n += 16
    pr.g.dma_start(out=c["one1_bf"][:], in_=i1).then_inc(pr.lw.h, 16)
    pr.lw.n += 16
    iE0 = pr.inp("C_E0", (P, P), F32)
    pr.g.dma_start(out=c["E0_bf"][:], in_=iE0).then_inc(pr.lw.h, 16)
    pr.lw.n += 16
    iOM = pr.inp("C_OM", (P, P), F32)
    pr.g.dma_start(out=c["onesm_bf"][:], in_=iOM).then_inc(pr.lw.h, 16)
    pr.lw.n += 16
    pr.vc(pr.v.memset(c["eps"][:], EPS))
    c["tok"] = (pr.ld.n, pr.lw.n)
    pr.t.wait_ge(pr.ld.h, pr.ld.n)
    pr.t.wait_ge(pr.lw.h, pr.lw.n)
    return c


def _const_arrays():
    e0 = np.zeros((P, P), np.float32)
    e0[0, :] = 1.0
    return {"C_I": np.eye(P, dtype=np.float32),
            "C_ones": np.ones((P, 1), np.float32),
            "C_one1": np.ones((1, P), np.float32),
            "C_E0": e0,
            "C_OM": np.ones((P, P), np.float32)}


def _rstd_newton(pr, rstd0, ve, tmp1, tmp2, out):
    v = pr.v
    pr.vc(v.tensor_tensor(tmp1[:], rstd0[:], rstd0[:], OP.mult))
    pr.vc(v.tensor_tensor(tmp1[:], tmp1[:], ve[:], OP.mult))
    pr.vc(v.tensor_scalar(tmp2[:], tmp1[:], -0.5, 1.5, OP.mult, OP.add))
    pr.vc(v.tensor_tensor(out[:], rstd0[:], tmp2[:], OP.mult))


def build_st1(n_tok):
    nt = n_tok // P
    pr = Prog("st1").__enter__()
    nc = pr.nc
    X = pr.inp("X", (n_tok, D), F32)
    XNT = pr.out("XNT", (D // P, nt, P, P), BF16)
    XT = pr.out("XT", (D // P, nt, P, P), F32)
    c = _const_inputs(pr)

    xb = [pr.sb(f"x{i}", (P, D), F32) for i in range(2)]
    xnb = [pr.sb(f"xn{i}", (P, D), BF16) for i in range(2)]
    stats = pr.sb("stats", (P, 6 * (D // 512)), F32)
    mv = pr.sb("mv", (P, 2), F32)
    ve = pr.sb("ve", (P, 1), F32)
    r0 = pr.sb("r0", (P, 1), F32)
    t1 = pr.sb("t1", (P, 1), F32)
    t2 = pr.sb("t2", (P, 1), F32)
    rstd = pr.sb("rstd", (P, 1), F32)
    nmr = pr.sb("nmr", (P, 1), F32)
    evb = [pr.sb(f"ev{i}", (P, 512), BF16) for i in range(2)]
    evf = [pr.sb(f"evf{i}", (P, 512), F32) for i in range(2)]
    psb = [pr.ps(f"psb{i}", (P, 512)) for i in range(2)]
    psf = [pr.ps(f"psf{i}", (P, 512)) for i in range(2)]

    ld_tok = [0] * nt
    done_tok = [0] * nt
    for tt in range(nt):
        b = tt % 2
        if tt >= 2:
            pr.sy.wait_ge(pr.st.h, done_tok[tt - 2])
        pr.ld.inc(pr.sy.dma_start(out=xb[b][:], in_=X[tt * P:(tt + 1) * P, :]), 16)
        ld_tok[tt] = pr.ld.n

        v = pr.v
        v.wait_ge(pr.ld.h, ld_tok[tt])
        for i in range(D // 512):
            v.bn_stats(stats[:, 6 * i:6 * i + 6], xb[b][:, 512 * i:512 * (i + 1)])
        pr.vc(v.bn_aggr(mv[:], stats[:]))
        pr.vc(v.tensor_scalar(ve[:], mv[:, 1:2], EPS, None, OP.add))
        a = pr.a
        a.wait_ge(pr.vv.h, pr.vv.n)
        pr.ach(a.activation(t1[:], mv[:, 1:2], AF.Ln, bias=c["eps"][:], scale=1.0))
        pr.ach(a.activation(r0[:], t1[:], AF.Exp, bias=0.0, scale=-0.5))
        v.wait_ge(pr.ac.h, pr.ac.n)
        _rstd_newton(pr, r0, ve, t1, t2, rstd)
        pr.vc(v.scalar_tensor_tensor(nmr[:], mv[:, 0:1], -1.0, rstd[:], OP.mult, OP.mult))
        pr.vc(v.tensor_scalar(xnb[b][:], xb[b][:], rstd[:], nmr[:], OP.mult, OP.add))
        xn_tok = pr.vv.n

        t = pr.t
        if tt == 0:
            st_tokb = [0, 0]
        for grp in range(D // 512):
            gb = grp % 2
            t.wait_ge(pr.vv.h, xn_tok)
            t.wait_ge(pr.st.h, st_tokb[gb])
            for j in range(4):
                fc = grp * 4 + j
                t.matmul(psb[gb][:, 128 * j:128 * (j + 1)], xnb[b][:, 128 * fc:128 * (fc + 1)],
                         c["I_bf"][:], start=(j == 0), stop=(j == 3))
            mm_tok = pr.mm.inc(t.matmul(psf[gb][:, 0:128], xb[b][:, 128 * (grp * 4):128 * (grp * 4) + 128],
                                        c["I_f"][:], start=True, stop=False))
            for j in range(1, 4):
                fc = grp * 4 + j
                mm_tok = pr.mm.inc(t.matmul(psf[gb][:, 128 * j:128 * (j + 1)],
                                            xb[b][:, 128 * fc:128 * (fc + 1)],
                                            c["I_f"][:], start=False, stop=(j == 3)))
            a = pr.a
            a.wait_ge(pr.mm.h, mm_tok)
            a.wait_ge(pr.st.h, st_tokb[gb])
            ev_tok = pr.ac.inc(a.copy(evb[gb][:], psb[gb][:]))
            v = pr.v
            v.wait_ge(pr.mm.h, mm_tok)
            v.wait_ge(pr.st.h, st_tokb[gb])
            evf_tok = pr.vv.inc(v.tensor_copy(evf[gb][:], psf[gb][:]))
            sy = pr.sy
            sy.wait_ge(pr.ac.h, ev_tok)
            sy.wait_ge(pr.vv.h, evf_tok)
            for j in range(4):
                fc = grp * 4 + j
                pr.st.inc(sy.dma_start(out=XNT[fc, tt], in_=evb[gb][:, 128 * j:128 * (j + 1)]), 16)
                pr.st.inc(sy.dma_start(out=XT[fc, tt], in_=evf[gb][:, 128 * j:128 * (j + 1)]), 16)
            st_tokb[gb] = pr.st.n
        done_tok[tt] = pr.st.n
    pr.finish()
    return pr


def build_st2(n_tok, dbg=False):
    nt = n_tok // P
    ng = n_tok // 512
    pr = Prog("st2").__enter__()
    XNT = pr.inp("XNT", (D // P, nt, P, P), BF16)
    WKD = pr.inp("WKD", (8, 32, P, P), F32)
    WUPK = pr.inp("WUPK", (4, P, 512), F32)
    WUPV = pr.inp("WUPV", (4, P, 1024), F32)
    COSK = pr.inp("COSK", (P, n_tok), F32)
    SINKS = pr.inp("SINKS", (P, n_tok), F32)
    KO = pr.out("KO", (NKV, P, n_tok), BF16)
    VO = pr.out("VO", (NKV, nt, P, P), BF16)
    KN2 = pr.out("KN2", (1, NKV), F32)
    CKV = pr.out("CKV", (4, P, n_tok), BF16) if dbg else None
    c = _const_inputs(pr)

    xn = [pr.sb(f"xn{k}", (P, n_tok), BF16) for k in range(D // P)]
    wb = [pr.sb(f"wb{i}", (P, 32 * P), BF16) for i in range(2)]
    ckv = [pr.sb(f"ckv{i}", (P, n_tok), BF16) for i in range(4)]
    cosk = pr.sb("cosk", (P, n_tok), F32)
    sinks = pr.sb("sinks", (P, n_tok), F32)
    krf = pr.sb("krf", (P, 512), F32)
    krs = pr.sb("krs", (P, 512), F32)
    ta = pr.sb("ta", (P, 512), F32)
    tb = pr.sb("tb", (P, 512), F32)
    kout = pr.sb("kout", (P, 512), BF16)
    wupk = pr.sb("wupk", (P, 4 * 512), BF16)
    wupv = pr.sb("wupv", (P, 4 * 1024), BF16)
    vout = pr.sb("vout", (P, 1024), BF16)
    sqb = pr.sb("sqb", (P, 1024), BF16)
    krow = pr.sb("krow", (1, 1), F32)
    kn_sb = pr.sb("kn_sb", (1, NKV), F32)
    kback = pr.sb("kback", (P, n_tok), BF16)
    ps_p = [pr.ps(f"ps_p{i}", (P, 512)) for i in range(2)]
    ps_v = [pr.ps(f"ps_v{i}", (P, 512)) for i in range(2)]
    ps_n = pr.ps("ps_n", (1, 512))

    sy, g, v, a, t = pr.sy, pr.g, pr.v, pr.a, pr.t

    for k in range(D // P):
        pr.ld.inc(sy.dma_start(out=xn[k][:].rearrange("p (t c) -> p t c", t=nt),
                               in_=XNT[k].rearrange("t p c -> p t c")), 16)
    pr.ld.inc(sy.dma_start(out=cosk[:], in_=COSK), 16)
    pr.ld.inc(sy.dma_start(out=sinks[:], in_=SINKS), 16)
    pr.lw.inc(g.dma_start(out=wupk[:].rearrange("p (a c) -> p a c", a=4),
                          in_=WUPK.rearrange("a p c -> p a c")), 16)
    pr.lw.inc(g.dma_start(out=wupv[:].rearrange("p (a c) -> p a c", a=4),
                          in_=WUPV.rearrange("a p c -> p a c")), 16)
    ld_all = pr.ld.n
    lw_small = pr.lw.n
    pr.vc(v.memset(kn_sb[:], 0.0))

    wtok = [0, 0]
    mm_done = [0, 0]
    for oc in range(8):
        wbb = oc % 2
        if oc >= 2:
            g.wait_ge(pr.mm.h, mm_done[wbb])
        pr.lw.inc(g.dma_start(out=wb[wbb][:].rearrange("p (k c) -> p k c", k=32),
                              in_=WKD[oc].rearrange("k p c -> p k c")), 16)
        w_tok = pr.lw.n
        for tg in range(ng):
            pp = tg % 2
            t.wait_ge(pr.ld.h, ld_all)
            t.wait_ge(pr.lw.h, w_tok)
            t.wait_ge(pr.ac.h, pr.ac.n)
            for k in range(32):
                inst = t.matmul(ps_p[pp][:], wb[wbb][:, P * k:P * (k + 1)],
                                xn[k][:, 512 * tg:512 * (tg + 1)], start=(k == 0), stop=(k == 31))
            mm_tok = pr.mm.inc(inst)
            mm_done[wbb] = pr.mm.n
            if oc < 4:
                a.wait_ge(pr.mm.h, mm_tok)
                et = pr.ac.inc(a.copy(krf[:], ps_p[pp][:]))
                sy.wait_ge(pr.ac.h, et)
                sh0 = pr.ld.n
                for (d0, s0) in ((0, 32), (32, 0), (64, 96), (96, 64)):
                    pr.ld.inc(sy.dma_start(out=krs[d0:d0 + 32, :], in_=krf[s0:s0 + 32, 0:512]), 16)
                sh_tok = pr.ld.n
                v.wait_ge(pr.ac.h, et)
                pr.vc(v.tensor_tensor(ta[:], krf[:], cosk[:, 512 * tg:512 * (tg + 1)], OP.mult))
                v.wait_ge(pr.ld.h, sh_tok)
                pr.vc(v.tensor_tensor(tb[:], krs[:], sinks[:, 512 * tg:512 * (tg + 1)], OP.mult))
                pr.vc(v.tensor_tensor(kout[:], ta[:], tb[:], OP.add))
                ko_tok = pr.vv.n
                sy.wait_ge(pr.vv.h, ko_tok)
                pr.st.inc(sy.dma_start(out=KO[2 * oc, 0:64, 512 * tg:512 * (tg + 1)],
                                       in_=kout[0:64, :]), 16)
                pr.st.inc(sy.dma_start(out=KO[2 * oc + 1, 0:64, 512 * tg:512 * (tg + 1)],
                                       in_=kout[64:128, :]), 16)
                a.wait_ge(pr.ld.h, sh_tok)
                a.wait_ge(pr.vv.h, ko_tok)
            else:
                a.wait_ge(pr.mm.h, mm_tok)
                pr.ac.inc(a.copy(ckv[oc - 4][:, 512 * tg:512 * (tg + 1)], ps_p[pp][:]))
    ckv_tok = pr.ac.n
    if dbg:
        sy.wait_ge(pr.ac.h, ckv_tok)
        for cc in range(4):
            pr.st.inc(sy.dma_start(out=CKV[cc], in_=ckv[cc][:]), 16)

    for oc2 in range(4):
        for tg in range(ng):
            pp = tg % 2
            t.wait_ge(pr.ac.h, pr.ac.n)
            t.wait_ge(pr.lw.h, lw_small)
            for cc in range(4):
                inst = t.matmul(ps_p[pp][:], wupk[:, 512 * cc + P * oc2: 512 * cc + P * (oc2 + 1)],
                                ckv[cc][:, 512 * tg:512 * (tg + 1)], start=(cc == 0), stop=(cc == 3))
            mm_tok = pr.mm.inc(inst)
            a.wait_ge(pr.mm.h, mm_tok)
            et = pr.ac.inc(a.copy(kout[:], ps_p[pp][:]))
            sy.wait_ge(pr.ac.h, et)
            pr.st.inc(sy.dma_start(out=KO[2 * oc2, 64:128, 512 * tg:512 * (tg + 1)],
                                   in_=kout[0:64, :]), 16)
            pr.st.inc(sy.dma_start(out=KO[2 * oc2 + 1, 64:128, 512 * tg:512 * (tg + 1)],
                                   in_=kout[64:128, :]), 16)
            a.wait_ge(pr.st.h, pr.st.n)

    for tt in range(nt):
        tg, tsub = tt // 4, tt % 4
        for half in range(2):
            hb = half
            t.wait_ge(pr.ac.h, pr.ac.n)
            for cc in range(4):
                inst = t.matmul(ps_v[hb][:], ckv[cc][:, P * tt:P * (tt + 1)],
                                wupv[:, 1024 * cc + 512 * half:1024 * cc + 512 * (half + 1)],
                                start=(cc == 0), stop=(cc == 3))
            mm_tok = pr.mm.inc(inst)
            a.wait_ge(pr.mm.h, mm_tok)
            pr.ac.inc(a.copy(vout[:, 512 * half:512 * (half + 1)], ps_v[hb][:]))
        sy.wait_ge(pr.ac.h, pr.ac.n)
        for h in range(NKV):
            pr.st.inc(sy.dma_start(out=VO[h, tt], in_=vout[:, P * h:P * (h + 1)]), 16)
        a.wait_ge(pr.st.h, pr.st.n)

    sq_tok = 0
    for h in range(NKV):
        sy.wait_ge(pr.ac.h, sq_tok)
        pr.ld.inc(sy.dma_start(out=kback[:], in_=KO[h]), 16)
        a.wait_ge(pr.ld.h, pr.ld.n)
        sq_tok = pr.ac.inc(a.activation(sqb[:, 0:n_tok], kback[:], AF.Square, bias=0.0, scale=1.0))
        for tg in range(ng):
            t.wait_ge(pr.ac.h, sq_tok)
            if ng > 0:
                t.wait_ge(pr.vv.h, pr.vv.n)
            mm_tok = pr.mm.inc(t.matmul(ps_n[:], c["ones_bf"][:],
                                        sqb[:, 512 * tg:512 * (tg + 1)], start=True, stop=True))
            v.wait_ge(pr.mm.h, mm_tok)
            pr.vc(v.tensor_reduce(krow[:], ps_n[:], axis=AX, op=OP.max))
            pr.vc(v.tensor_tensor(kn_sb[:, h:h + 1], kn_sb[:, h:h + 1], krow[:], OP.max))
        a.wait_ge(pr.vv.h, pr.vv.n)
    v.wait_ge(pr.vv.h, pr.vv.n)
    fin = pr.vv.inc(v.tensor_copy(kn_sb[:], kn_sb[:]))
    sy.wait_ge(pr.vv.h, fin)
    pr.st.inc(sy.dma_start(out=KN2, in_=kn_sb[:]), 16)
    pr.finish()
    return pr


def build_st3(n_tok):
    nt = n_tok // P
    ng = n_tok // 512
    pr = Prog("st3").__enter__()
    XNT = pr.inp("XNT", (D // P, nt, P, P), BF16)
    WQ = pr.inp("WQ", (32, 32, P, P), F32)
    COSQ = pr.inp("COSQ", (P, n_tok), F32)
    SINQS = pr.inp("SINQS", (P, n_tok), F32)
    QO = pr.out("QO", (NQH, P, n_tok), BF16)
    QNR = pr.out("QNR", (NQH, n_tok), F32)
    c = _const_inputs(pr)

    xn = [pr.sb(f"xn{k}", (P, n_tok), BF16) for k in range(D // P)]
    wb = [pr.sb(f"wb{i}", (P, 32 * P), BF16) for i in range(2)]
    cosq = pr.sb("cosq", (P, n_tok), F32)
    sinqs = pr.sb("sinqs", (P, n_tok), F32)
    qf = pr.sb("qf", (P, 512), F32)
    qs = pr.sb("qs", (P, 512), F32)
    ta = pr.sb("ta", (P, 512), F32)
    tb = pr.sb("tb", (P, 512), F32)
    qbf = pr.sb("qbf", (P, 512), BF16)
    sqb = pr.sb("sqb", (P, 512), BF16)
    qnrow = pr.sb("qnrow", (1, 512), F32)
    ps_p = [pr.ps(f"ps_p{i}", (P, 512)) for i in range(2)]
    ps_n = pr.ps("ps_n", (1, 512))

    sy, g, v, a, t = pr.sy, pr.g, pr.v, pr.a, pr.t
    for k in range(D // P):
        pr.ld.inc(sy.dma_start(out=xn[k][:].rearrange("p (t c) -> p t c", t=nt),
                               in_=XNT[k].rearrange("t p c -> p t c")), 16)
    pr.ld.inc(sy.dma_start(out=cosq[:], in_=COSQ), 16)
    pr.ld.inc(sy.dma_start(out=sinqs[:], in_=SINQS), 16)
    ld_all = pr.ld.n

    mm_done = [0, 0]
    for oc in range(NQH):
        wbb = oc % 2
        if oc >= 2:
            g.wait_ge(pr.mm.h, mm_done[wbb])
        pr.lw.inc(g.dma_start(out=wb[wbb][:].rearrange("p (k c) -> p k c", k=32),
                              in_=WQ[oc].rearrange("k p c -> p k c")), 16)
        w_tok = pr.lw.n
        for tg in range(ng):
            pp = tg % 2
            t.wait_ge(pr.ld.h, ld_all)
            t.wait_ge(pr.lw.h, w_tok)
            t.wait_ge(pr.ac.h, pr.ac.n)
            for k in range(32):
                inst = t.matmul(ps_p[pp][:], wb[wbb][:, P * k:P * (k + 1)],
                                xn[k][:, 512 * tg:512 * (tg + 1)], start=(k == 0),
                                stop=(k == 31))
            mm_tok = pr.mm.inc(inst)
            mm_done[wbb] = pr.mm.n
            a.wait_ge(pr.mm.h, mm_tok)
            et = pr.ac.inc(a.copy(qf[:], ps_p[pp][:]))
            sy.wait_ge(pr.ac.h, et)
            for (d0, s0) in ((0, 32), (32, 0), (64, 96), (96, 64)):
                pr.ld.inc(sy.dma_start(out=qs[d0:d0 + 32, :], in_=qf[s0:s0 + 32, 0:512]), 16)
            sh_tok = pr.ld.n
            v.wait_ge(pr.ac.h, et)
            pr.vc(v.tensor_tensor(ta[:], qf[:], cosq[:, 512 * tg:512 * (tg + 1)], OP.mult))
            v.wait_ge(pr.ld.h, sh_tok)
            pr.vc(v.tensor_tensor(tb[:], qs[:], sinqs[:, 512 * tg:512 * (tg + 1)], OP.mult))
            pr.vc(v.tensor_tensor(qbf[:], ta[:], tb[:], OP.add))
            q_tok = pr.vv.n
            sy.wait_ge(pr.vv.h, q_tok)
            pr.st.inc(sy.dma_start(out=QO[oc][:, 512 * tg:512 * (tg + 1)], in_=qbf[:]), 16)
            a.wait_ge(pr.vv.h, q_tok)
            sq_tok = pr.ac.inc(a.activation(sqb[:], qbf[:], AF.Square, bias=0.0, scale=1.0))
            t.wait_ge(pr.ac.h, sq_tok)
            t.wait_ge(pr.vv.h, pr.vv.n)
            nm_tok = pr.mm.inc(t.matmul(ps_n[:], c["ones_bf"][:], sqb[:], start=True, stop=True))
            v.wait_ge(pr.mm.h, nm_tok)
            v.wait_ge(pr.st.h, pr.st.n)
            qn_tok = pr.vv.inc(v.tensor_copy(qnrow[:], ps_n[:]))
            sy.wait_ge(pr.vv.h, qn_tok)
            pr.st.inc(sy.dma_start(out=QNR[oc:oc + 1, 512 * tg:512 * (tg + 1)], in_=qnrow[:]), 16)
            a.wait_ge(pr.st.h, pr.st.n)
    pr.finish()
    return pr


SHIFT = 55.0


def _att_heads(i):
    return [i, i + 8, i + 16, i + 24]


def build_st4():
    pr = Prog("st4").__enter__()
    Q4 = pr.inp("Q4", (4, P, S), BF16)
    K4 = pr.inp("K4", (4, P, S), BF16)
    V4 = pr.inp("V4", (4, S // P, P, P), BF16)
    QNR4X = pr.inp("QNR4X", (P, S // 32), F32)
    KN4X = pr.inp("KN4X", (P, NCORE), F32)
    TRIADD = pr.inp("TRIADD", (P, P), F32)
    O4 = pr.out("O4", (4 * P, S), BF16)
    SUMD = pr.out("SUMD", (4, S), F32)
    NEGD = pr.out("NEGD", (4, S), BF16)
    NEGSC = pr.nc.dram_tensor("NEGSC", [4, S], BF16, kind="Internal").ap()
    c = _const_inputs(pr)

    sy, g, v, a, t = pr.sy, pr.g, pr.v, pr.a, pr.t
    q_sb = [pr.sb(f"q{j}", (P, S), BF16) for j in range(4)]
    kb = [pr.sb(f"kb{i}", (P, S), BF16) for i in range(2)]
    vb = [pr.sb(f"vb{i}", (P, S), BF16) for i in range(2)]
    triadd = pr.sb("triadd", (P, P), BF16)
    qnrx = pr.sb("qnrx", (P, S // 32), F32)
    kn4x = pr.sb("kn4x", (P, NCORE), F32)
    kn128 = pr.sb("kn128", (P, 1), F32)
    kn128_b = pr.sb("kn128_b", (P, 1), F32)
    negc128 = pr.sb("negc128", (P, S // 32), BF16)
    negc_use = [pr.sb(f"ncu{i}", (P, BS), BF16) for i in range(2)]
    pt = [pr.sb(f"pt{i}", (P, 1024), BF16) for i in range(2)]
    sinv128 = pr.sb("sinv128", (P, BS), F32)
    obuf = pr.sb("obuf", (P, BS), BF16)
    ps_s = [pr.ps(f"ps_s{i}", (P, 1024)) for i in range(2)]
    ps_o = pr.ps("ps_o", (P, BS))
    ps_sum = pr.ps("ps_sum", (P, BS))
    ps_bc = pr.ps("ps_bc", (P, BS))

    for j in range(4):
        pr.ld.inc(sy.dma_start(out=q_sb[j][:], in_=Q4[j]), 16)
    pr.ld.inc(sy.dma_start(out=kn4x[:], in_=KN4X), 16)
    pr.ld.inc(sy.dma_start(out=qnrx[:], in_=QNR4X), 16)
    pr.lw.inc(g.dma_start(out=triadd[:], in_=TRIADD), 16)
    ld_base = pr.ld.n
    lw_base = pr.lw.n

    qb = pr.sb("qb", (P, S // 32), F32)
    pr.vc(v.memset(negc_use[0][:], 0.0))
    pr.vc(v.memset(negc_use[1][:], 0.0))
    v.wait_ge(pr.ld.h, ld_base)
    pr.vc(v.tensor_reduce(kn128[:], kn4x[:], axis=AX, op=OP.max))
    pr.vc(v.tensor_copy(kn128_b[:], kn128[:]))
    pr.vv.inc(v.drain())
    a.wait_ge(pr.vv.h, pr.vv.n)
    pr.ach(a.activation(qb[:], qnrx[:], AF.Sqrt, bias=0.0, scale=kn128_b[:]))
    pr.ach(a.activation(negc128[:], qb[:], AF.Copy, bias=SHIFT / SCALE, scale=-1.01))
    negc_tok = pr.ac.n
    sy.wait_ge(pr.ac.h, negc_tok)
    pr.st.inc(sy.dma_start(out=NEGD.rearrange("j (th c) -> (j th) c", th=32),
                           in_=negc128[:]), 16)
    pr.ld.inc(sy.dma_start(out=NEGSC.rearrange("j (th c) -> (j th) c", th=32),
                           in_=negc128[:]), 16)
    negsc_tok = pr.ld.n

    gunits = []
    for idx, (b, j) in enumerate([(b, j) for b in range(NB) for j in range(4)]):
        units = [("pair", 2 * kp, 2 * kp + 1) for kp in range(2 * b)]
        units += [("diag", 4 * b + jj, jj) for jj in range(4)]
        for ui, u in enumerate(units):
            gunits.append((idx, b, j, u, ui == len(units) - 1))

    kv_tok = {}
    kv_done = {}
    ncu_mm = [0, 0]
    ncu_tok = {}
    pt_last = [0, 0]
    e_toks = {}
    qk_toks = {}
    loaded_idx = -1

    def load_bj(idx, b, j):
        if idx == 0:
            sy.wait_ge(pr.ld.h, negsc_tok)
        buf = idx % 2
        kend = BS * (b + 1)
        nkc = kend // P
        if idx >= 2:
            sy.wait_ge(pr.mm.h, kv_done.get(idx - 2, 0))
        pr.ld.inc(sy.dma_start(out=kb[buf][:, 0:kend], in_=K4[j][:, 0:kend]), 16)
        pr.ld.inc(sy.dma_start(out=vb[buf][:, 0:kend].rearrange("p (t c) -> p t c", t=nkc),
                               in_=V4[j, 0:nkc].rearrange("t p c -> p t c")), 16)
        nb = idx % 2
        sy.wait_ge(pr.mm.h, ncu_mm[nb])
        pr.ld.inc(sy.dma_start(out=negc_use[nb][0:1, :],
                               in_=NEGSC[j:j + 1, b * BS:(b + 1) * BS]), 16)
        kv_tok[idx] = pr.ld.n

    G = len(gunits)
    for g in range(G + 1):
        if g < G:
            idx, b, j, unit, is_last = gunits[g]
            buf = idx % 2
            nb = idx % 2
            pp = g % 2
            if idx > loaded_idx:
                load_bj(idx, b, j)
                loaded_idx = idx
                if idx == 0:
                    t.wait_ge(pr.ac.h, negc_tok)
                    t.wait_ge(pr.lw.h, lw_base)
                    t.wait_ge(pr.ld.h, negsc_tok)
            t.wait_ge(pr.ld.h, kv_tok[idx])
            t.wait_ge(pr.ac.h, e_toks.get(g - 2, 0))
            if unit[0] == "pair":
                k0, k1 = unit[1], unit[2]
                for sl, kc in ((0, k0), (1, k1)):
                    t.matmul(ps_s[pp][:, 512 * sl:512 * (sl + 1)],
                             kb[buf][:, P * kc:P * (kc + 1)],
                             q_sb[j][:, b * BS:(b + 1) * BS], start=True, stop=False)
                    inst = t.matmul(ps_s[pp][:, 512 * sl:512 * (sl + 1)],
                                    c["E0_bf"][:], negc_use[nb][:],
                                    start=False, stop=True)
            else:
                kc, jj = unit[1], unit[2]
                ncols = BS - P * jj
                t.matmul(ps_s[pp][:, 0:ncols], kb[buf][:, P * kc:P * (kc + 1)],
                         q_sb[j][:, b * BS + P * jj:(b + 1) * BS], start=True, stop=False)
                t.matmul(ps_s[pp][:, 0:ncols], c["E0_bf"][:],
                         negc_use[nb][:, P * jj:BS], start=False, stop=False)
                inst = t.matmul(ps_s[pp][:, 0:P], c["I_bf"][:], triadd[:],
                                start=False, stop=True)
            inst.then_inc(pr.mm.h, 1); pr.mm.n += 1
            qk_toks[g] = pr.mm.inc(t.drain())
            if is_last:
                ncu_mm[nb] = qk_toks[g]
        else:
            t.wait_ge(pr.ac.h, e_toks.get(g - 2, 0))
            qk_toks[g] = pr.mm.inc(t.matmul(ps_bc[:, 0:P], c["I_bf"][:], triadd[:],
                                            start=True, stop=True))

        if g >= 1:
            hgl = g - 1
            idx1, b1, j1, unit1, is_last1 = gunits[hgl]
            buf1 = idx1 % 2
            pp1 = hgl % 2
            a.wait_ge(pr.mm.h, qk_toks[g])
            a.wait_ge(pr.mm.h, pt_last[pp1])
            if unit1[0] == "pair":
                e_toks[hgl] = pr.ac.inc(a.activation(pt[pp1][:], ps_s[pp1][:], AF.Exp,
                                                     bias=0.0, scale=SCALE))
            else:
                ncols1 = BS - P * unit1[2]
                e_toks[hgl] = pr.ac.inc(a.activation(pt[pp1][:, 0:ncols1],
                                                     ps_s[pp1][:, 0:ncols1],
                                                     AF.Exp, bias=0.0, scale=SCALE))
            t.wait_ge(pr.ac.h, e_toks[hgl])
            if unit1[0] == "pair":
                k0, k1 = unit1[1], unit1[2]
                for sl, kc in ((0, k0), (1, k1)):
                    t.matmul(ps_sum[:], c["onesm_bf"][:], pt[pp1][:, 512 * sl:512 * (sl + 1)],
                             start=(kc == 0), stop=False, skip_group_check=True)
                    inst = t.matmul(ps_o[:], vb[buf1][:, P * kc:P * (kc + 1)],
                                    pt[pp1][:, 512 * sl:512 * (sl + 1)],
                                    start=(kc == 0), stop=False, skip_group_check=True)
            else:
                kc, jj = unit1[1], unit1[2]
                ncols1 = BS - P * jj
                t.matmul(ps_sum[:, P * jj:BS], c["onesm_bf"][:], pt[pp1][:, 0:ncols1],
                         start=(kc == 0), stop=False, skip_group_check=True)
                inst = t.matmul(ps_o[:, P * jj:BS], vb[buf1][:, P * kc:P * (kc + 1)],
                                pt[pp1][:, 0:ncols1], start=(kc == 0), stop=False,
                                skip_group_check=True)
            inst.then_inc(pr.mm.h, 1); pr.mm.n += 1
            sum_av = pr.mm.inc(t.drain())
            pt_last[pp1] = sum_av
            if is_last1:
                kv_done[idx1] = sum_av
                dr_tok = pr.mm.inc(t.drain())
                v.wait_ge(pr.mm.h, dr_tok)
                v.wait_ge(pr.st.h, pr.st.n)
                pr.vc(v.reciprocal(sinv128[:], ps_sum[:]))
                pr.vc(v.tensor_tensor(obuf[:], ps_o[:], sinv128[:], OP.mult))
                pr.vv.inc(v.drain())
                o_tok = pr.vv.n
                sy.wait_ge(pr.vv.h, o_tok)
                pr.st.inc(sy.dma_start(out=O4[j1 * P:(j1 + 1) * P, b1 * BS:(b1 + 1) * BS],
                                       in_=obuf[:]), 16)
                pr.st.inc(sy.dma_start(out=SUMD[j1:j1 + 1, b1 * BS:(b1 + 1) * BS],
                                       in_=sinv128[0:1, :]), 16)
                t.wait_ge(pr.vv.h, o_tok)
    pr.finish()
    return pr


def build_st5(n_tok):
    nt = n_tok // P
    nB = n_tok // BS
    pr = Prog("st5").__enter__()
    XNT = pr.inp("XNT", (D // P, nt, P, P), BF16)
    WBASE = pr.inp("WBASE", (32, 32, P, P), F32)
    BB = pr.inp("BB", (P, 32), F32)
    WGATE = pr.inp("WGATE", (32, P, 8), F32)
    A0T = pr.inp("A0T", (32, P, P), F32)
    B0T = pr.inp("B0T", (P, D), F32)
    G = pr.out("G", (D, n_tok), F32)
    c = _const_inputs(pr)
    sy, g, v, a, t = pr.sy, pr.g, pr.v, pr.a, pr.t

    xn = [pr.sb(f"xn{k}", (P, n_tok), BF16) for k in range(D // P)]
    wgate = pr.sb("wgate", (P, 32 * 8), BF16)
    a0t = pr.sb("a0t", (P, 32 * P), BF16)
    b0t = pr.sb("b0t", (P, D), BF16)
    bb = pr.sb("bb", (P, 32), F32)
    wb = [pr.sb(f"wb{i}", (P, 32 * P), BF16) for i in range(2)]
    mx = pr.sb("mx", (P, 1), F32)
    nmx = pr.sb("nmx", (P, 1), F32)
    el = pr.sb("el", (P, 8), F32)
    mask1 = pr.sb("mask1", (P, 8), F32)
    t2 = pr.sb("t2", (P, 8), F32)
    mx2 = pr.sb("mx2", (P, 1), F32)
    keep = pr.sb("keep", (P, 8), F32)
    ww = pr.sb("ww", (P, 8), F32)
    den = pr.sb("den", (P, 1), F32)
    rd = pr.sb("rd", (P, 1), F32)
    route = pr.sb("route", (P, 8), BF16)
    hwt = pr.sb("hwt", (P, P), BF16)
    hw_f = pr.sb("hw_f", (P, 2 * BS), BF16)
    gbufs = [pr.sb(f"gb{i}", (P, BS), F32) for i in range(2)]
    ps_lg = pr.ps("ps_lg", (P, 8))
    ps_h = pr.ps("ps_h", (P, P))
    ps_t = pr.ps("ps_t", (P, P))
    ps_o = [pr.ps(f"ps_o{i}", (P, BS)) for i in range(2)]

    for k in range(D // P):
        pr.ld.inc(sy.dma_start(out=xn[k][:].rearrange("p (t c) -> p t c", t=nt),
                               in_=XNT[k].rearrange("t p c -> p t c")), 16)
    pr.ld.inc(sy.dma_start(out=bb[:], in_=BB), 16)
    pr.lw.inc(g.dma_start(out=wgate[:].rearrange("p (k e) -> p k e", k=32),
                          in_=WGATE.rearrange("k p e -> p k e")), 16)
    pr.lw.inc(g.dma_start(out=a0t[:].rearrange("p (k e) -> p k e", k=32),
                          in_=A0T.rearrange("k p e -> p k e")), 16)
    pr.lw.inc(g.dma_start(out=b0t[:], in_=B0T), 16)
    ld_base, lw_base = pr.ld.n, pr.lw.n

    hw_toks = []
    for Bi in range(nB):
        hw_tok_end = 0
        for tt4 in range(4):
            ttg = Bi * 4 + tt4
            t.wait_ge(pr.ld.h, ld_base)
            t.wait_ge(pr.lw.h, lw_base)
            t.wait_ge(pr.vv.h, pr.vv.n)
            t.wait_ge(pr.ac.h, pr.ac.n)
            for k in range(32):
                t.matmul(ps_lg[:], xn[k][:, P * ttg:P * (ttg + 1)],
                         wgate[:, 8 * k:8 * (k + 1)], start=(k == 0), stop=(k == 31),
                         skip_group_check=True)
                inst = t.matmul(ps_h[:], xn[k][:, P * ttg:P * (ttg + 1)],
                                a0t[:, P * k:P * (k + 1)], start=(k == 0), stop=(k == 31),
                                skip_group_check=True)
            mm_tok = pr.mm.inc(inst)
            v.wait_ge(pr.mm.h, mm_tok)
            pr.vc(v.tensor_reduce(mx[:], ps_lg[:], axis=AX, op=OP.max))
            pr.vc(v.tensor_scalar(nmx[:], mx[:], -1.0, None, OP.mult))
            pr.vv.inc(v.drain())
            a.wait_ge(pr.vv.h, pr.vv.n)
            e_tok = pr.ac.inc(a.activation(el[:], ps_lg[:], AF.Exp, bias=nmx[:], scale=1.0))
            v.wait_ge(pr.ac.h, e_tok)
            pr.vc(v.tensor_reduce(mx[:], el[:], axis=AX, op=OP.max))
            pr.vc(v.tensor_scalar(mask1[:], el[:], mx[:], None, OP.is_ge))
            pr.vc(v.scalar_tensor_tensor(t2[:], mask1[:], -2.0, el[:], OP.mult, OP.add))
            pr.vc(v.tensor_reduce(mx2[:], t2[:], axis=AX, op=OP.max))
            pr.vc(v.tensor_scalar(keep[:], el[:], mx2[:], None, OP.is_ge))
            pr.vc(v.tensor_tensor(ww[:], el[:], keep[:], OP.mult))
            pr.vc(v.tensor_scalar(ww[:], ww[:], 1.0, None, OP.mult, OP.add, accum_out=den[:]))
            pr.vc(v.reciprocal(rd[:], den[:]))
            pr.vc(v.tensor_scalar(route[:], ww[:], rd[:], None, OP.mult))
            pr.vc(v.tensor_tensor(
                hwt[:].rearrange("p (e r) -> p e r", e=8),
                ps_h[:].rearrange("p (e r) -> p e r", e=8),
                route[:].unsqueeze(-1).broadcast_to((P, 8, 16)), OP.mult))
            pr.vv.inc(v.drain())
            hw_tok = pr.vv.n
            t.wait_ge(pr.vv.h, hw_tok)
            tr_tok = pr.mm.inc(t.matmul(ps_t[:], hwt[:], c["I_bf"][:], start=True, stop=True))
            a.wait_ge(pr.mm.h, tr_tok)
            hw_tok_end = pr.ac.inc(a.copy(hw_f[:, BS * Bi + P * tt4:BS * Bi + P * (tt4 + 1)], ps_t[:]))
        hw_toks.append(hw_tok_end)

    w_tok = [0, 0]
    oc_done = [0, 0]
    for oc in range(32):
        wbb = oc % 2
        if oc >= 2:
            g.wait_ge(pr.mm.h, oc_done[wbb])
        pr.lw.inc(g.dma_start(out=wb[wbb][:].rearrange("p (k c) -> p k c", k=32),
                              in_=WBASE[oc].rearrange("k p c -> p k c")), 16)
        w_tok[wbb] = pr.lw.n
        for Bi in range(nB):
            po = (oc * nB + Bi) % 2
            t.wait_ge(pr.lw.h, w_tok[wbb])
            t.wait_ge(pr.ac.h, pr.ac.n)
            for k in range(32):
                t.matmul(ps_o[po][:], wb[wbb][:, P * k:P * (k + 1)],
                         xn[k][:, BS * Bi:BS * (Bi + 1)], start=(k == 0), stop=False,
                         skip_group_check=True)
            inst = t.matmul(ps_o[po][:], b0t[:, P * oc:P * (oc + 1)],
                            hw_f[:, BS * Bi:BS * (Bi + 1)], start=False, stop=True,
                            skip_group_check=True)
            mm_tok = pr.mm.inc(inst)
            oc_done[wbb] = pr.mm.n
            a.wait_ge(pr.mm.h, mm_tok)
            a.wait_ge(pr.st.h, pr.st.n)
            g_tok = pr.ac.inc(a.activation(gbufs[po][:], ps_o[po][:], AF.Silu,
                                           bias=bb[:, oc:oc + 1], scale=1.0))
            sy.wait_ge(pr.ac.h, g_tok)
            pr.st.inc(sy.dma_start(out=G[P * oc:P * (oc + 1), BS * Bi:BS * (Bi + 1)],
                                   in_=gbufs[po][:]), 16)
    pr.finish()
    return pr


def build_st6(n_tok):
    nt = n_tok // P
    nB = n_tok // BS
    pr = Prog("st6").__enter__()
    OI = pr.inp("OI", (D, n_tok), BF16)
    WOS = pr.inp("WOS", (32, 32, P, P), F32)
    GI = pr.inp("GI", (D, n_tok), F32)
    XT = pr.inp("XT", (D // P, nt, P, P), F32)
    X1 = pr.out("X1", (D, n_tok), F32)
    XN1T = pr.out("XN1T", (D // P, nt, P, P), BF16)
    c = _const_inputs(pr)
    sy, g, v, a, t = pr.sy, pr.g, pr.v, pr.a, pr.t

    ob = [pr.sb(f"ob{k}", (P, n_tok), BF16) for k in range(D // P)]
    wb = [pr.sb(f"wb{i}", (P, 32 * P), BF16) for i in range(2)]
    gch = [pr.sb(f"gch{i}", (P, BS), F32) for i in range(2)]
    xch = [pr.sb(f"xch{i}", (P, BS), F32) for i in range(2)]
    x1c = [pr.sb(f"x1c{i}", (P, BS), F32) for i in range(2)]
    sqc = pr.sb("sqc", (P, BS), F32)
    onesm_f = pr.sb("onesm_f", (P, P), F32)
    mrow = pr.sb("mrow", (P, BS), F32)
    ex2 = pr.sb("ex2", (P, BS), F32)
    var = pr.sb("var", (P, BS), F32)
    ve = pr.sb("ve", (P, BS), F32)
    r0 = pr.sb("r0", (P, BS), F32)
    t1 = pr.sb("t1", (P, BS), F32)
    t2b = pr.sb("t2b", (P, BS), F32)
    rstd = pr.sb("rstd", (P, BS), F32)
    mrs = pr.sb("mrs", (P, BS), F32)
    xn1c = pr.sb("xn1c", (P, BS), BF16)
    ps_s = [pr.ps(f"ps_s{i}", (P, BS)) for i in range(2)]
    ps_sx = pr.ps("ps_sx", (P, BS))
    ps_sq = pr.ps("ps_sq", (P, BS))

    for k in range(D // P):
        pr.ld.inc(sy.dma_start(out=ob[k][:], in_=OI[P * k:P * (k + 1), :]), 16)
    iOM = pr.inp("C_OMX", (P, P), F32)
    pr.ld.inc(sy.dma_start(out=onesm_f[:], in_=iOM), 16)
    ld_base = pr.ld.n

    for Bi in range(nB):
        w_tok = [0, 0]
        oc_done = [0, 0]
        st_x1 = {}
        for oc in range(32):
            wbb = oc % 2
            if oc >= 2 or Bi > 0:
                g.wait_ge(pr.mm.h, oc_done[wbb] if oc >= 2 else pr.mm.n)
            pr.lw.inc(g.dma_start(out=wb[wbb][:].rearrange("p (k c) -> p k c", k=32),
                                  in_=WOS[oc].rearrange("k p c -> p k c")), 16)
            w_tok[wbb] = pr.lw.n
            ch = oc % 2
            sy.wait_ge(pr.vv.h, pr.vv.n)
            pr.ld.inc(sy.dma_start(out=gch[ch][:], in_=GI[P * oc:P * (oc + 1), BS * Bi:BS * (Bi + 1)]), 16)
            pr.ld.inc(sy.dma_start(
                out=xch[ch][:].rearrange("p (t c) -> p t c", t=4),
                in_=XT[oc, 4 * Bi:4 * Bi + 4].rearrange("t p c -> p t c")), 16)
            ch_tok = pr.ld.n
            t.wait_ge(pr.ld.h, ch_tok)
            t.wait_ge(pr.lw.h, w_tok[wbb])
            t.wait_ge(pr.vv.h, pr.vv.n)
            for k in range(32):
                inst = t.matmul(ps_s[ch][:], wb[wbb][:, P * k:P * (k + 1)],
                                ob[k][:, BS * Bi:BS * (Bi + 1)], start=(k == 0),
                                stop=(k == 31))
            mm_tok = pr.mm.inc(inst)
            oc_done[wbb] = pr.mm.n
            v.wait_ge(pr.mm.h, mm_tok)
            v.wait_ge(pr.ld.h, ch_tok)
            v.wait_ge(pr.st.h, pr.st.n)
            pr.vc(v.tensor_tensor(x1c[ch][:], ps_s[ch][:], gch[ch][:], OP.mult))
            pr.vc(v.tensor_tensor(x1c[ch][:], x1c[ch][:], xch[ch][:], OP.add))
            pr.vv.inc(v.drain())
            x1_tok = pr.vv.n
            a.wait_ge(pr.vv.h, x1_tok)
            sq_tok = pr.ac.inc(a.activation(sqc[:], x1c[ch][:], AF.Square, bias=0.0, scale=1.0))
            sy.wait_ge(pr.vv.h, x1_tok)
            pr.st.inc(sy.dma_start(out=X1[P * oc:P * (oc + 1), BS * Bi:BS * (Bi + 1)],
                                   in_=x1c[ch][:]), 16)
            t.wait_ge(pr.vv.h, x1_tok)
            t.wait_ge(pr.ld.h, ld_base)
            t.matmul(ps_sx[:], onesm_f[:], x1c[ch][:], start=(oc == 0), stop=False,
                     skip_group_check=True)
            t.wait_ge(pr.ac.h, sq_tok)
            inst = t.matmul(ps_sq[:], onesm_f[:], sqc[:], start=(oc == 0), stop=False,
                            skip_group_check=True)
            pr.mm.inc(inst)
        st_tok = pr.mm.inc(t.drain())
        v.wait_ge(pr.mm.h, st_tok)
        pr.vc(v.tensor_scalar(mrow[:], ps_sx[:], 1.0 / D, None, OP.mult))
        pr.vc(v.tensor_scalar(ex2[:], ps_sq[:], 1.0 / D, None, OP.mult))
        pr.vc(v.tensor_tensor(t1[:], mrow[:], mrow[:], OP.mult))
        pr.vc(v.tensor_tensor(var[:], ex2[:], t1[:], OP.subtract))
        pr.vc(v.tensor_scalar(ve[:], var[:], EPS, None, OP.add))
        pr.vv.inc(v.drain())
        a.wait_ge(pr.vv.h, pr.vv.n)
        pr.ach(a.activation(t1[:], var[:], AF.Ln, bias=c["eps"][:], scale=1.0))
        pr.ach(a.activation(r0[:], t1[:], AF.Exp, bias=0.0, scale=-0.5))
        v.wait_ge(pr.ac.h, pr.ac.n)
        _rstd_newton(pr, r0, ve, t1, t2b, rstd)
        pr.vc(v.scalar_tensor_tensor(mrs[:], mrow[:], -1.0, rstd[:], OP.mult, OP.mult))
        pr.vv.inc(v.drain())
        ln_tok = pr.vv.n
        for oc in range(32):
            ch = oc % 2
            sy.wait_ge(pr.vv.h, pr.vv.n)
            pr.ld.inc(sy.dma_start(out=x1c[ch][:], in_=X1[P * oc:P * (oc + 1), BS * Bi:BS * (Bi + 1)]), 16)
            ch_tok = pr.ld.n
            v.wait_ge(pr.ld.h, ch_tok)
            v.wait_ge(pr.vv.h, ln_tok)
            v.wait_ge(pr.st.h, pr.st.n)
            pr.vc(v.tensor_tensor(t2b[:], x1c[ch][:], rstd[:], OP.mult))
            pr.vc(v.tensor_tensor(xn1c[:], t2b[:], mrs[:], OP.add))
            pr.vv.inc(v.drain())
            ap_tok = pr.vv.n
            sy.wait_ge(pr.vv.h, ap_tok)
            for tt4 in range(4):
                pr.st.inc(sy.dma_start(out=XN1T[oc, 4 * Bi + tt4],
                                       in_=xn1c[:, P * tt4:P * (tt4 + 1)]), 16)
    pr.finish()
    return pr


def build_st7():
    pr = Prog("st7").__enter__()
    MROWS = pr.inp("MROWS", (512, D), I32)
    BMP = pr.out("BMP", (P, 16), BF16)
    c = _const_inputs(pr)
    sy, g, v, a, t = pr.sy, pr.g, pr.v, pr.a, pr.t
    G32 = pr.inp("G32", (P, 4), F32)
    msk = [pr.sb(f"msk{i}", (P, D), I32) for i in range(2)]
    tmp = pr.sb("tmp", (P, P), F32)
    tmp2 = pr.sb("tmp2", (P, P), F32)
    g32 = pr.sb("g32", (P, 4), F32)
    bmp = pr.sb("bmp", (P, 16), BF16)
    ps_b = pr.ps("ps_b", (P, 16))
    pr.ld.inc(sy.dma_start(out=g32[:], in_=G32), 16)
    mm_tok = 0
    for rt in range(4):
        ch = rt % 2
        if rt >= 2:
            sy.wait_ge(pr.vv.h, pr.vv.n)
        pr.ld.inc(sy.dma_start(out=msk[ch][:], in_=MROWS[P * rt:P * (rt + 1), :]), 16)
        m_tok = pr.ld.n
        v.wait_ge(pr.ld.h, m_tok)
        v.wait_ge(pr.mm.h, mm_tok)
        pr.vc(v.tensor_reduce(tmp[:], msk[ch][:].rearrange("p (cb w) -> p cb w", w=32),
                              axis=AX, op=OP.add))
        pr.vc(v.tensor_copy(tmp2[:], tmp[:]))
        pr.vv.inc(v.drain())
        r_tok = pr.vv.n
        t.wait_ge(pr.vv.h, r_tok)
        t.wait_ge(pr.ld.h, pr.ld.n)
        inst = t.matmul(ps_b[:, 4 * rt:4 * (rt + 1)], tmp2[:], g32[:],
                        start=(rt == 0), stop=(rt == 3), skip_group_check=True)
        mm_tok = pr.mm.inc(inst)
    fin_tok = pr.mm.inc(t.drain())
    v.wait_ge(pr.mm.h, fin_tok)
    pr.vc(v.tensor_scalar(bmp[:], ps_b[:], 0.0, None, OP.is_gt))
    pr.vv.inc(v.drain())
    b_tok = pr.vv.n
    sy.wait_ge(pr.vv.h, b_tok)
    pr.st.inc(sy.dma_start(out=BMP, in_=bmp[:]), 16)
    pr.finish()
    return pr


def build_st8(n_tok):
    nt = n_tok // P
    nB = n_tok // BS
    pr = Prog("st8").__enter__()
    XN1T = pr.inp("XN1T", (D // P, nt, P, P), BF16)
    X1 = pr.inp("X1", (D, n_tok), F32)
    L1W = pr.inp("L1W", (8, 32, P, BS), F32)
    L1BW = pr.inp("L1BW", (8, 32, P, BS), F32)
    BMT = pr.inp("BMT", (P, P), BF16)
    SEL = pr.inp("SEL", (P, 32 * P), F32)
    L1AT = pr.inp("L1AT", (32, P, 64), F32)
    ROUT = pr.inp("ROUT", (32, P, 16), F32)
    L1BT = pr.inp("L1BT", (8, P, D), F32)
    BIAS2 = pr.inp("BIAS2", (P, 64), F32)
    X2 = pr.out("X2", (D, n_tok), F32)
    c = _const_inputs(pr)
    sy, g, v, a, t = pr.sy, pr.g, pr.v, pr.a, pr.t

    xn = [pr.sb(f"xn{k}", (P, n_tok), BF16) for k in range(D // P)]
    l1bt = pr.sb("l1bt", (P, 8 * 512), BF16)
    l1at = pr.sb("l1at", (P, 32 * 64), BF16)
    rout = pr.sb("rout", (P, 32 * 16), BF16)
    sel = pr.sb("sel", (P, 32 * P), BF16)
    bmt = pr.sb("bmt", (P, P), BF16)
    bias2 = pr.sb("bias2", (P, 64), F32)
    bmx = pr.sb("bmx", (P, 32 * P), BF16)
    wm = pr.sb("wm", (P, 32 * BS), BF16)
    bw = pr.sb("bw", (P, 32 * BS), BF16)
    mx = pr.sb("mx", (P, 1), F32)
    nmx = pr.sb("nmx", (P, 1), F32)
    el = pr.sb("el", (P, 16), F32)
    denf = pr.sb("denf", (P, 1), F32)
    rdf = pr.sb("rdf", (P, 1), F32)
    pp_ = pr.sb("pp_", (P, 16), F32)
    mask1 = pr.sb("mask1", (P, 16), F32)
    t2 = pr.sb("t2", (P, 16), F32)
    mx2 = pr.sb("mx2", (P, 1), F32)
    keep = pr.sb("keep", (P, 16), F32)
    ww = pr.sb("ww", (P, 16), F32)
    den = pr.sb("den", (P, 1), F32)
    rd = pr.sb("rd", (P, 1), F32)
    comb = pr.sb("comb", (P, 16), BF16)
    hwt = pr.sb("hwt", (P, 1024), BF16)
    hw_f = pr.sb("hw_f", (P, 2 * 8 * BS), BF16)
    g1 = pr.sb("g1", (P, BS), F32)
    x1c = [pr.sb(f"x1c{i}", (P, BS), F32) for i in range(2)]
    x2c = pr.sb("x2c", (P, BS), F32)
    ps_lg = pr.ps("ps_lg", (P, 16))
    ps_rh = pr.ps("ps_rh", (P, 64))
    ps_t = pr.ps("ps_t", (P, BS))
    ps_f = [pr.ps(f"ps_f{i}", (P, BS)) for i in range(2)]

    for k in range(D // P):
        pr.ld.inc(sy.dma_start(out=xn[k][:].rearrange("p (t c) -> p t c", t=nt),
                               in_=XN1T[k].rearrange("t p c -> p t c")), 16)
    pr.ld.inc(sy.dma_start(out=bmt[:], in_=BMT), 16)
    pr.ld.inc(sy.dma_start(out=bias2[:], in_=BIAS2), 16)
    pr.lw.inc(g.dma_start(out=l1at[:].rearrange("p (k c) -> p k c", k=32),
                          in_=L1AT.rearrange("k p c -> p k c")), 16)
    pr.lw.inc(g.dma_start(out=rout[:].rearrange("p (k c) -> p k c", k=32),
                          in_=ROUT.rearrange("k p c -> p k c")), 16)
    pr.lw.inc(g.dma_start(out=sel[:], in_=SEL), 16)
    ld_base, lw_base = pr.ld.n, pr.lw.n

    t.wait_ge(pr.ld.h, ld_base)
    t.wait_ge(pr.lw.h, lw_base)
    for kq in range(8):
        t.wait_ge(pr.ac.h, pr.ac.n)
        for j in range(4):
            k = 4 * kq + j
            inst = t.matmul(ps_t[:, P * j:P * (j + 1)], sel[:, P * k:P * (k + 1)],
                            bmt[:], start=(j == 0), stop=(j == 3), skip_group_check=True)
        mm_tok = pr.mm.inc(inst)
        a.wait_ge(pr.mm.h, mm_tok)
        pr.ac.inc(a.copy(bmx[:, BS * kq:BS * (kq + 1)], ps_t[:]))
    bmx_tok = pr.ac.n

    for Bi in range(nB):
        for tt4 in range(4):
            ttg = Bi * 4 + tt4
            t.wait_ge(pr.vv.h, pr.vv.n)
            t.wait_ge(pr.ac.h, pr.ac.n)
            for k in range(32):
                t.matmul(ps_lg[:], xn[k][:, P * ttg:P * (ttg + 1)],
                         rout[:, 16 * k:16 * (k + 1)], start=(k == 0), stop=(k == 31),
                         skip_group_check=True)
                inst = t.matmul(ps_rh[:], xn[k][:, P * ttg:P * (ttg + 1)],
                                l1at[:, 64 * k:64 * (k + 1)], start=(k == 0), stop=(k == 31),
                                skip_group_check=True)
            mm_tok = pr.mm.inc(inst)
            v.wait_ge(pr.mm.h, mm_tok)
            pr.vc(v.tensor_reduce(mx[:], ps_lg[:], axis=AX, op=OP.max))
            pr.vc(v.tensor_scalar(nmx[:], mx[:], -1.0, None, OP.mult))
            pr.vv.inc(v.drain())
            a.wait_ge(pr.vv.h, pr.vv.n)
            e_tok = pr.ac.inc(a.activation(el[:], ps_lg[:], AF.Exp, bias=nmx[:], scale=1.0))
            v.wait_ge(pr.ac.h, e_tok)
            pr.vc(v.tensor_scalar(el[:], el[:], 1.0, None, OP.mult, OP.add, accum_out=denf[:]))
            pr.vc(v.reciprocal(rdf[:], denf[:]))
            pr.vc(v.tensor_scalar(pp_[:], el[:], rdf[:], None, OP.mult))
            pr.vc(v.tensor_reduce(mx[:], pp_[:], axis=AX, op=OP.max))
            pr.vc(v.tensor_scalar(mask1[:], pp_[:], mx[:], None, OP.is_ge))
            pr.vc(v.scalar_tensor_tensor(t2[:], mask1[:], -2.0, pp_[:], OP.mult, OP.add))
            pr.vc(v.tensor_reduce(mx2[:], t2[:], axis=AX, op=OP.max))
            pr.vc(v.tensor_scalar(keep[:], pp_[:], mx2[:], None, OP.is_ge))
            pr.vc(v.tensor_tensor(ww[:], pp_[:], keep[:], OP.mult))
            pr.vc(v.tensor_scalar(ww[:], ww[:], 1.0, None, OP.mult, OP.add, accum_out=den[:]))
            pr.vc(v.tensor_scalar(den[:], den[:], 1e-6, None, OP.add))
            pr.vc(v.reciprocal(rd[:], den[:]))
            pr.vc(v.tensor_scalar(comb[:], ww[:], rd[:], 0.5, OP.mult, OP.mult))
            pr.vc(v.tensor_tensor(
                hwt[:].rearrange("p (e r) -> p e r", e=16),
                ps_rh[:].unsqueeze(1).broadcast_to((P, 16, 64)),
                comb[:].unsqueeze(-1).broadcast_to((P, 16, 64)), OP.mult))
            pr.vv.inc(v.drain())
            hw_tok = pr.vv.n
            t.wait_ge(pr.vv.h, hw_tok)
            for eq in range(2):
                t.wait_ge(pr.ac.h, pr.ac.n)
                for j in range(4):
                    ec = 4 * eq + j
                    inst = t.matmul(ps_t[:, P * j:P * (j + 1)],
                                    hwt[:, P * ec:P * (ec + 1)], c["I_bf"][:],
                                    start=(j == 0), stop=(j == 3), skip_group_check=True)
                tr_tok = pr.mm.inc(inst)
                a.wait_ge(pr.mm.h, tr_tok)
                for j in range(4):
                    ec = 4 * eq + j
                    pr.ac.inc(a.copy(hw_f[:, (Bi * 8 + ec) * BS + P * tt4:
                                           (Bi * 8 + ec) * BS + P * (tt4 + 1)],
                                     ps_t[:, P * j:P * (j + 1)]))
    hwf_tok = pr.ac.n

    prev_og_mm = 0
    for og in range(8):
        g.wait_ge(pr.mm.h, prev_og_mm)
        pr.lw.inc(g.dma_start(out=l1bt[:].rearrange("p (e c) -> p e c", e=8),
                              in_=L1BT[:, :, BS * og:BS * (og + 1)].rearrange("e p c -> p e c")), 16)
        wk_tok = [0] * 32
        for k in range(32):
            g.wait_ge(pr.vv.h, pr.vv.n)
            g.wait_ge(pr.mm.h, prev_og_mm)
            pr.lw.inc(g.dma_start(out=wm[:, BS * k:BS * (k + 1)], in_=L1W[og, k]), 16)
            pr.lw.inc(g.dma_start(out=bw[:, BS * k:BS * (k + 1)], in_=L1BW[og, k]), 16)
            wl_tok = pr.lw.n
            v.wait_ge(pr.lw.h, wl_tok)
            v.wait_ge(pr.ac.h, bmx_tok)
            pr.vc(v.tensor_tensor(
                wm[:, BS * k:BS * (k + 1)].rearrange("p (rb w) -> p rb w", rb=16),
                wm[:, BS * k:BS * (k + 1)].rearrange("p (rb w) -> p rb w", rb=16),
                bmx[:, P * k + og * 16:P * k + (og + 1) * 16]
                .unsqueeze(-1).broadcast_to((P, 16, 32)), OP.mult))
            wk_tok[k] = pr.vv.n
        pr.vv.inc(v.drain())
        wm_tok = pr.vv.n
        og_lw = pr.lw.n
        for Bi in range(nB):
            for oc in range(4):
                po = oc % 2
                o_glob = og * 4 + oc
                t.wait_ge(pr.vv.h, wm_tok)
                t.wait_ge(pr.lw.h, og_lw)
                t.wait_ge(pr.ac.h, pr.ac.n)
                t.wait_ge(pr.vv.h, pr.vv.n)
                for k in range(32):
                    t.matmul(ps_f[po][:], bw[:, BS * k + P * oc:BS * k + P * (oc + 1)],
                             xn[k][:, BS * Bi:BS * (Bi + 1)], start=(k == 0), stop=False,
                             skip_group_check=True)
                for ec in range(8):
                    inst = t.matmul(ps_f[po][:],
                                    l1bt[:, BS * ec + P * oc:BS * ec + P * (oc + 1)],
                                    hw_f[:, (Bi * 8 + ec) * BS:(Bi * 8 + ec + 1) * BS],
                                    start=False, stop=(ec == 7), skip_group_check=True)
                f_tok = pr.mm.inc(inst)
                a.wait_ge(pr.mm.h, f_tok)
                g_tok = pr.ac.inc(a.activation(g1[:], ps_f[po][:], AF.Silu,
                                               bias=bias2[:, 32 + o_glob:33 + o_glob], scale=1.0))
                t.wait_ge(pr.ac.h, g_tok)
                for k in range(32):
                    inst = t.matmul(ps_f[po][:], wm[:, BS * k + P * oc:BS * k + P * (oc + 1)],
                                    xn[k][:, BS * Bi:BS * (Bi + 1)], start=(k == 0),
                                    stop=(k == 31), skip_group_check=True)
                s_tok = pr.mm.inc(inst)
                ch = (Bi * 4 + oc) % 2
                sy.wait_ge(pr.vv.h, pr.vv.n)
                pr.ld.inc(sy.dma_start(out=x1c[ch][:],
                                       in_=X1[P * o_glob:P * (o_glob + 1), BS * Bi:BS * (Bi + 1)]), 16)
                x_tok = pr.ld.n
                v.wait_ge(pr.mm.h, s_tok)
                v.wait_ge(pr.ld.h, x_tok)
                v.wait_ge(pr.ac.h, g_tok)
                v.wait_ge(pr.st.h, pr.st.n)
                pr.vc(v.scalar_tensor_tensor(x2c[:], ps_f[po][:],
                                             bias2[:, o_glob:o_glob + 1], g1[:],
                                             OP.add, OP.mult))
                pr.vc(v.tensor_tensor(x2c[:], x2c[:], x1c[ch][:], OP.add))
                pr.vv.inc(v.drain())
                x2_tok = pr.vv.n
                sy.wait_ge(pr.vv.h, x2_tok)
                pr.st.inc(sy.dma_start(out=X2[P * o_glob:P * (o_glob + 1), BS * Bi:BS * (Bi + 1)],
                                       in_=x2c[:]), 16)
        prev_og_mm = pr.mm.n
    pr.finish()
    return pr


def _host_prep(inputs):
    H = {}
    H["CA"] = _const_arrays()
    cosK, sinKS, cosQ, sinQS = _rope_tables()
    H["cosK"], H["sinKS"], H["cosQ"], H["sinQS"] = cosK, sinKS, cosQ, sinQS
    H["WKD"] = np.concatenate([_stream_oc(inputs["L0_attn_Wkr"]),
                               _stream_oc(inputs["L0_attn_Wdk"])], 0)
    H["WUPK"] = np.ascontiguousarray(inputs["L0_attn_Wupk"].T.reshape(4, P, 512))
    H["WUPV"] = np.ascontiguousarray(inputs["L0_attn_Wupv"].T.reshape(4, P, 1024))
    H["WQ"] = _stream_oc(_wq_perm_T(inputs["L0_attn_Wq"]).T)
    H["TRIADD"] = np.where(np.arange(P)[None, :] >= np.arange(P)[:, None],
                           0.0, -30000.0).astype(np.float32)
    H["WBASE"] = _stream_oc(inputs["L0_ffn_W_base"])
    H["BB"] = np.ascontiguousarray(inputs["L0_ffn_b_base"].reshape(32, P).T)
    H["WGATE"] = np.ascontiguousarray(inputs["L0_ffn_W_gate"].T.reshape(32, P, 8))
    H["A0T"] = np.ascontiguousarray(inputs["L0_ffn_A"].reshape(P, D).T.reshape(32, P, P))
    H["B0T"] = np.ascontiguousarray(inputs["L0_ffn_B"].transpose(0, 2, 1).reshape(P, D))
    H["WOS"] = _stream_oc(inputs["L0_attn_Wo"])
    G32 = np.zeros((P, 4), np.float32)
    for r in range(P):
        G32[r, r // 32] = 1.0
    H["G32"] = G32
    H["L1W"] = np.ascontiguousarray(
        inputs["L1_attn_weight"].reshape(8, 512, 32, P).transpose(0, 2, 3, 1))
    H["L1BW"] = np.ascontiguousarray(
        inputs["L1_ffn_base_w"].reshape(8, 512, 32, P).transpose(0, 2, 3, 1))
    SEL = np.zeros((P, 32 * P), np.float32)
    for k in range(32):
        for p in range(P):
            SEL[4 * k + p // 32, P * k + p] = 1.0
    H["SEL"] = SEL
    H["L1AT"] = np.ascontiguousarray(inputs["L1_ffn_A"].T.reshape(32, P, 64))
    H["ROUT"] = np.ascontiguousarray(inputs["L1_ffn_router_w"].T.reshape(32, P, 16))
    H["L1BT"] = np.ascontiguousarray(
        inputs["L1_ffn_B"].transpose(0, 2, 1).reshape(1024, D).reshape(8, P, D))
    H["BIAS2"] = np.concatenate([inputs["L1_attn_bias"].reshape(32, P).T,
                                 inputs["L1_ffn_base_b"].reshape(32, P).T], 1).astype(np.float32)
    H["OMX"] = np.ones((P, P), np.float32)
    return H


def kernel(**inputs):
    inputs = {k: np.ascontiguousarray(v) for k, v in inputs.items()}
    x = inputs["L0_attn_hidden_states"][0]
    H = _host_prep(inputs)
    CA = H["CA"]
    own = [_own_rows(i) for i in range(NCORE)]
    NT = 1024
    cores = list(range(NCORE))

    pr = build_st1(NT)
    r1 = run_bass_kernel_spmd(pr.nc, [dict(X=np.ascontiguousarray(x[own[i]]), **CA)
                                      for i in cores], core_ids=cores).results
    XNT = [r1[i]["XNT"] for i in cores]
    XT = [r1[i]["XT"] for i in cores]

    pr = build_st2(NT)
    r2 = run_bass_kernel_spmd(pr.nc, [dict(
        XNT=XNT[i], WKD=H["WKD"], WUPK=H["WUPK"], WUPV=H["WUPV"],
        COSK=np.ascontiguousarray(H["cosK"][:, own[i]]),
        SINKS=np.ascontiguousarray(H["sinKS"][:, own[i]]), **CA) for i in cores],
        core_ids=cores).results

    pr = build_st3(NT)
    r3 = run_bass_kernel_spmd(pr.nc, [dict(
        XNT=XNT[i], WQ=H["WQ"],
        COSQ=np.ascontiguousarray(H["cosQ"][:, own[i]]),
        SINQS=np.ascontiguousarray(H["sinQS"][:, own[i]]), **CA) for i in cores],
        core_ids=cores).results

    KF = np.zeros((NKV, P, S), ml_dtypes.bfloat16)
    VF = np.zeros((NKV, S // P, P, P), ml_dtypes.bfloat16)
    QF = np.zeros((NQH, P, S), ml_dtypes.bfloat16)
    QNR = np.zeros((NQH, S), np.float32)
    for i in cores:
        KF[:, :, own[i]] = r2[i]["KO"]
        a, b = _own_blocks(i)
        VF[:, 4 * a:4 * a + 4] = r2[i]["VO"][:, 0:4]
        VF[:, 4 * b:4 * b + 4] = r2[i]["VO"][:, 4:8]
        QF[:, :, own[i]] = r3[i]["QO"]
        QNR[:, own[i]] = r3[i]["QNR"]
    KN8 = np.stack([r2[i]["KN2"][0] for i in cores], 1).astype(np.float32)

    pr = build_st4()
    m4 = []
    for i in cores:
        hs = _att_heads(i)
        kv = [qh // 4 for qh in hs]
        m4.append(dict(Q4=np.ascontiguousarray(QF[hs]), K4=np.ascontiguousarray(KF[kv]),
                       V4=np.ascontiguousarray(VF[kv]),
                       QNR4X=np.ascontiguousarray(QNR[hs].reshape(P, S // 32)),
                       KN4X=np.ascontiguousarray(np.repeat(KN8[kv], 32, axis=0)),
                       TRIADD=H["TRIADD"], **CA))
    r4 = run_bass_kernel_spmd(pr.nc, m4, core_ids=cores).results
    OFULL = np.zeros((D, S), ml_dtypes.bfloat16)
    for i in cores:
        for j in range(4):
            qh = i + 8 * j
            OFULL[qh * P:(qh + 1) * P] = r4[i]["O4"][j * P:(j + 1) * P]

    pr = build_st5(NT)
    r5 = run_bass_kernel_spmd(pr.nc, [dict(
        XNT=XNT[i], WBASE=H["WBASE"], BB=H["BB"], WGATE=H["WGATE"], A0T=H["A0T"],
        B0T=H["B0T"], **CA) for i in cores], core_ids=cores).results

    pr = build_st6(NT)
    r6 = run_bass_kernel_spmd(pr.nc, [dict(
        OI=np.ascontiguousarray(OFULL[:, own[i]]), WOS=H["WOS"], GI=r5[i]["G"],
        XT=XT[i], C_OMX=H["OMX"], **CA) for i in cores], core_ids=cores).results

    pr = build_st7()
    r7 = run_bass_kernel_spmd(pr.nc, [dict(
        MROWS=np.ascontiguousarray(inputs["L1_attn_mask"][512 * i:512 * (i + 1)]),
        G32=H["G32"], **CA) for i in cores], core_ids=cores).results
    BMT = np.concatenate([r7[i]["BMP"] for i in cores], 1)

    pr = build_st8(NT)
    r8 = run_bass_kernel_spmd(pr.nc, [dict(
        XN1T=r6[i]["XN1T"], X1=r6[i]["X1"], L1W=H["L1W"], L1BW=H["L1BW"], BMT=BMT,
        SEL=H["SEL"], L1AT=H["L1AT"], ROUT=H["ROUT"], L1BT=H["L1BT"],
        BIAS2=H["BIAS2"], **CA) for i in cores], core_ids=cores).results

    out = np.zeros((S, D), np.float32)
    for i in cores:
        out[own[i]] = r8[i]["X2"].T
    return out[None]
